# version 05
# baseline (speedup 1.0000x reference)
import numpy as np
import concourse.bass as bass
import concourse.mybir as mybir
from concourse.bass_utils import run_bass_kernel_spmd
from contextlib import ExitStack

F32 = mybir.dt.float32
BF16 = mybir.dt.bfloat16
AF = mybir.ActivationFunctionType
ALU = mybir.AluOpType
AX = mybir.AxisListType
AP = bass.AP

RNE = 12582912.0
B, S, D = 4, 2048, 2048
T = 1024
NT = T // 128
ND = D // 128
HID0 = 8192
EPS_LN = 1e-5
EPS_RMS0 = 1.1920929e-7

D_INNER, D_STATE, NH, HD, D_CONV, CHUNK = 4096, 128, 64, 64, 4, 256
D_XBC = D_INNER + 2 * D_STATE
D_PROJ = 2 * D_INNER + 2 * D_STATE + NH
NCH = T // CHUNK


def cdiv(a, b):
    return (a + b - 1) // b


class KB:
    ENGS = ("tensor", "vector", "scalar", "sync", "gpsimd")
    SEMS = ("dl", "dw", "dg", "dx", "sv", "ss", "st", "sgp", "scc")

    def __init__(self, nc):
        self.nc = nc
        self.q = {e: [] for e in self.ENGS}
        self.ct = {s: 0 for s in self.SEMS}
        self.sems = {}

    def push(self, eng, fn):
        self.q[eng].append(fn)

    def do(self, eng, meth, *a, **kw):
        self.push(eng, lambda e, m=meth, a=a, kw=kw: getattr(e, m)(*a, **kw))

    def doi(self, eng, sem, meth, *a, **kw):
        self.ct[sem] += 1
        self.push(eng, lambda e, m=meth, a=a, kw=kw, s=sem:
                  getattr(e, m)(*a, **kw).then_inc(self.sems[s], 1))
        return self.ct[sem]

    def wait(self, eng, sem, val=None):
        v = self.ct[sem] if val is None else val
        self.push(eng, lambda e, s=sem, vv=v: e.wait_ge(self.sems[s], vv))
        return v

    def sig(self, eng, sem):
        self.ct[sem] += 1
        self.push(eng, lambda e, s=sem: e.drain().then_inc(self.sems[s], 1))
        return self.ct[sem]

    def drain(self, eng):
        self.push(eng, lambda e: e.drain())

    def dma(self, eng, sem, out, in_, **kw):
        self.ct[sem] += 16
        self.push(eng, lambda e, o=out, i=in_, s=sem, kw=kw:
                  e.dma_start(out=o, in_=i, **kw).then_inc(self.sems[s], 16))
        return self.ct[sem]

    def cc(self, kind, op, groups, ins, outs):
        self.ct["scc"] += 1
        self.push("gpsimd", lambda e: e.collective_compute(
            kind, op, replica_groups=groups, ins=ins, outs=outs
        ).then_inc(self.sems["scc"], 1))
        return self.ct["scc"]

    def finalize(self, es):
        for name in self.SEMS:
            self.sems[name] = es.enter_context(self.nc.semaphore(name))
        block = es.enter_context(self.nc.Block())
        for eng_name in self.ENGS:
            ops = self.q[eng_name]
            getattr(block, eng_name)(lambda e, ops=ops: [f(e) for f in ops])


class Res:

    def __init__(self, k, ap):
        self.k = k
        self.ap = ap
        self.guard = None

    def acquire(self, eng):
        if self.guard is not None:
            self.k.wait(eng, self.guard[0], self.guard[1])

    def release(self, sem, count):
        self.guard = (sem, count)


SHARD_SIZES = dict(
    w_fcT=2048 * 1024,
    w_projT=16 * 1024 * 128,
    in_projT=256 * 8512,
    out_projT=512 * 2048,
    weightT=256 * 2048,
    wq=2048 * 256,
    wout=2048 * 256,
    small=2048 * 520,
)
OFF_WKV = 16384
OFF_AW = 540672
OFF_BW = 802816
SHARD_OFF = {}
_o = 0
for _k2, _v2 in SHARD_SIZES.items():
    SHARD_OFF[_k2] = _o
    _o += _v2
SHARD_N = _o


def build_core(n_sub=6, taps_req=(), dump=0):
    nc = bass.Bass("TRN2", target_bir_lowering=False)
    k = KB(nc)

    x_in = nc.dram_tensor("x", [T, D], F32, kind="ExternalInput")
    meta_in = nc.dram_tensor("core_meta", [1, 16], F32, kind="ExternalInput")
    wfc_in = nc.dram_tensor("wfc", [1024, 2048], F32, kind="ExternalInput")
    wproj_in = nc.dram_tensor("wproj", [2048, 1024], F32, kind="ExternalInput")
    inproj_in = nc.dram_tensor("inproj", [8512, 256], F32, kind="ExternalInput")
    outproj_in = nc.dram_tensor("outproj", [2048, 512], F32, kind="ExternalInput")
    weight_in = nc.dram_tensor("weight", [2048, 256], F32, kind="ExternalInput")
    wq_in = nc.dram_tensor("wq", [2048, 256], F32, kind="ExternalInput")
    wout_in = nc.dram_tensor("wout", [2048, 256], F32, kind="ExternalInput")
    aw_in = nc.dram_tensor("aw", [128, 2048], F32, kind="ExternalInput")
    gatew_in = nc.dram_tensor("gatew", [8, 2048], F32, kind="ExternalInput")
    wkv_in = nc.dram_tensor("wkv", [2048, 256], F32, kind="ExternalInput")
    bw_in = nc.dram_tensor("bw", [16384, 16], F32, kind="ExternalInput")
    gnorm_in = nc.dram_tensor("gnorm", [1, 2048], F32, kind="ExternalInput")
    gout_in = nc.dram_tensor("gout", [1, 2048], F32, kind="ExternalInput")
    normw_in = nc.dram_tensor("normw", [1, 512], F32, kind="ExternalInput")
    convw_in = nc.dram_tensor("convw", [4352, 4], F32, kind="ExternalInput")
    dtb_in = nc.dram_tensor("dtb", [64, 1], F32, kind="ExternalInput")
    alog_in = nc.dram_tensor("alog", [64, 1], F32, kind="ExternalInput")
    dpar_in = nc.dram_tensor("dpar", [64, 1], F32, kind="ExternalInput")

    out_t = nc.dram_tensor("out", [T, D], F32, kind="ExternalOutput")

    wblob_in = nc.dram_tensor("wblob_in", [SHARD_N], BF16)
    wblob = nc.dram_tensor("wblob", [8 * SHARD_N], BF16)
    dscr = nc.dram_tensor("dscr", [64 * 2080], F32)
    ccd_in = nc.dram_tensor("ccd_in", [2 * 131072], BF16)
    ccd_out = nc.dram_tensor("ccd_out", [2, 2 * 131072], BF16)
    ccm_in = nc.dram_tensor("ccm_in", [544 * 1024], F32)
    ccm_out = nc.dram_tensor("ccm_out", [2, 544 * 1024], F32)

    def blob_r(rank, name, r0, nr, c0, ncl, width):
        base = rank * SHARD_N + SHARD_OFF[name] + r0 * width + c0
        return AP(wblob.ap().tensor, base, [[width, nr], [1, ncl]])

    def blob_w(name, r0, nr, c0, ncl, width):
        base = SHARD_OFF[name] + r0 * width + c0
        return AP(wblob_in.ap().tensor, base, [[width, nr], [1, ncl]])

    def dscr_ap(off, pattern):
        return AP(dscr.ap().tensor, off, pattern)

    es = ExitStack()
    sb = {}

    def alloc(name, shape, dtype):
        sb[name] = es.enter_context(nc.sbuf_tensor(name, shape, dtype))
        return sb[name]

    alloc("xb", [128, NT * 2048], F32)
    alloc("xnf", [128, ND * 1024], BF16)
    alloc("gbuf", [128, 32768], BF16)
    alloc("wbuf", [128, 8192], BF16)
    alloc("stat", [128, 64], F32)
    alloc("rbc", [128, 1024], F32)
    alloc("idn", [128, 132], BF16)
    alloc("idnf", [128, 128], F32)
    alloc("prep", [128, 1024], F32)
    alloc("prepb", [128, 2048], BF16)
    alloc("prepc", [128, 2048], BF16)
    alloc("small", [128, 96], F32)
    alloc("mam", [128, 1024], F32)
    alloc("ystg", [128, 5120], BF16)

    pf = [Res(k, es.enter_context(nc.psum_tensor(f"pf{i}", [128, 512], F32))) for i in range(6)]
    pbk = [Res(k, es.enter_context(nc.psum_tensor(f"pb{i}", [128, 512], BF16))) for i in range(2)]
    stg = [Res(k, sb["prepc"][:, 1536 + 128 * i:1536 + 128 * (i + 1)]) for i in range(2)]
    sb["srow"] = sb["prep"]

    taps = {}

    def tap(name, ap):
        if name in taps_req:
            taps[name] = ap

    def xt(i):
        return sb["xb"][:, 2048 * i:2048 * (i + 1)]

    def xnfc(c, t0, tn):
        return sb["xnf"][:, 1024 * c + t0:1024 * c + t0 + tn]

    for i in range(NT):
        k.dma("sync", "dl", xt(i), x_in[128 * i:128 * (i + 1), :])
    k.dma("sync", "dl", sb["small"][:, 0:1], meta_in[0:1, 0:1].partition_broadcast(128))
    k.do("vector", "memset", sb["small"][:, 1:2], -16.0)
    k.do("vector", "memset", sb["idn"][:], 1.0)
    k.do("vector", "memset", sb["idnf"][:], 1.0)
    k.sig("vector", "sv")
    k.wait("gpsimd", "sv")
    k.doi("gpsimd", "sgp", "affine_select", out=sb["idn"][:, 0:128], in_=sb["idn"][:, 0:128],
          pattern=[[1, 128]], compare_op=ALU.is_equal, fill=0.0,
          base=0, channel_multiplier=-1)
    k.doi("gpsimd", "sgp", "affine_select", out=sb["idnf"][:], in_=sb["idnf"][:],
          pattern=[[1, 128]], compare_op=ALU.is_equal, fill=0.0,
          base=0, channel_multiplier=-1)
    SGP_IDN = k.ct["sgp"]
    k.wait("tensor", "sgp", SGP_IDN)

    def tp_evac_store(src_bf_ap, pcount, blk_writer, nblk, ident=None):
        ident = sb["idn"][0:pcount, 0:pcount] if ident is None else ident
        for j in range(nblk):
            bank = pbk[j % 2]
            stage = stg[j % 2]
            bank.acquire("tensor")
            stv = k.doi("tensor", "st", "transpose", bank.ap[:, 0:pcount],
                        src_bf_ap[0:pcount, 128 * j:128 * (j + 1)], ident)
            k.wait("scalar", "st", stv)
            stage.acquire("scalar")
            k.do("scalar", "copy", out=stage.ap[:, 0:pcount], in_=bank.ap[:, 0:pcount])
            ssv = k.sig("scalar", "ss")
            bank.release("ss", ssv)
            k.wait("sync", "ss", ssv)
            dgv = blk_writer(j, stage.ap[:, 0:pcount])
            stage.release("dg", dgv)

    def prep_ternary(src_dram, n_tiles, tile_cols, out_writer):
        ngrp = tile_cols // 64
        for t in range(n_tiles):
            wb = sb["prepb"][:, 0:tile_cols]
            k.wait("gpsimd", "st")
            k.dma("gpsimd", "dg", wb, src_dram[128 * t:128 * (t + 1), 0:tile_cols])
            k.wait("vector", "dg")
            k.do("vector", "reduce_sum", out=sb["stat"][:, 0:ngrp],
                 in_=wb.rearrange("p (g c) -> p g c", c=64), axis=AX.X,
                 apply_absolute_value=True)
            k.drain("vector")
            k.do("vector", "tensor_scalar", out=sb["stat"][:, 0:ngrp],
                 in0=sb["stat"][:, 0:ngrp], scalar1=1.0 / 64, scalar2=1e-8,
                 op0=ALU.mult, op1=ALU.max)
            k.drain("vector")
            sbf = sb["ystg"][:, 0:ngrp]
            k.do("vector", "tensor_copy", out=sbf, in_=sb["stat"][:, 0:ngrp])
            k.drain("vector")
            k.do("vector", "reciprocal", out=sb["stat"][:, 32:32 + ngrp], in_=sbf)
            k.drain("vector")
            tq = sb["prepc"][:, 0:tile_cols]
            k.do("vector", "tensor_tensor", out=tq, in0=wb,
                 in1=sb["stat"][:, 32:32 + ngrp].rearrange("p g -> p g ()").broadcast_to([128, ngrp, 64]),
                 op=ALU.mult)
            k.drain("vector")
            k.do("vector", "tensor_scalar", out=tq, in0=tq, scalar1=RNE, scalar2=RNE,
                 op0=ALU.add, op1=ALU.subtract)
            k.drain("vector")
            k.do("vector", "tensor_scalar", out=tq, in0=tq, scalar1=1.0, scalar2=-1.0,
                 op0=ALU.min, op1=ALU.max)
            k.drain("vector")
            k.do("vector", "tensor_tensor", out=wb, in0=tq,
                 in1=sbf.rearrange("p g -> p g ()").broadcast_to([128, ngrp, 64]),
                 op=ALU.mult)
            svv = k.sig("vector", "sv")
            k.wait("tensor", "sv", svv)
            tp_evac_store(wb, 128, lambda j, stage, t=t: out_writer(t, j, stage),
                          tile_cols // 128)

    prep_ternary(wfc_in, 8, 2048,
                 lambda t, j, stage: k.dma("sync", "dg",
                     blob_w("w_fcT", 128 * j, 128, 128 * t, 128, 1024), stage))

    def wproj_writer(t, j, stage):
        base = SHARD_OFF["w_projT"] + t * (1024 * 128) + (128 * j) * 128
        return k.dma("sync", "dg", AP(wblob_in.ap().tensor, base, [[128, 128], [1, 128]]), stage)
    prep_ternary(wproj_in, 16, 1024, wproj_writer)

    def prep_transpose_cast(src_dram, n_r, n_c, writer, scale_col=None):
        for ri in range(cdiv(n_r, 128)):
            pr = min(128, n_r - 128 * ri)
            wb = sb["prepb"][:, 0:n_c]
            k.wait("gpsimd", "st")
            k.dma("gpsimd", "dg", wb[0:pr, :], src_dram[128 * ri:128 * ri + pr, 0:n_c])
            k.wait("tensor", "dg")
            for j in range(n_c // 128):
                bank = pbk[j % 2]
                stage = stg[j % 2]
                bank.acquire("tensor")
                stv = k.doi("tensor", "st", "transpose", bank.ap[:, 0:pr],
                            wb[0:pr, 128 * j:128 * (j + 1)], sb["idn"][0:pr, 0:pr])
                k.wait("scalar", "st", stv)
                stage.acquire("scalar")
                dst = stage.ap[:, 0:pr]
                if scale_col is not None:
                    k.do("scalar", "activation", out=dst, in_=bank.ap[:, 0:pr],
                         func=AF.Copy, scale=scale_col(j))
                else:
                    k.do("scalar", "copy", out=dst, in_=bank.ap[:, 0:pr])
                ssv = k.sig("scalar", "ss")
                bank.release("ss", ssv)
                k.wait("sync", "ss", ssv)
                dgv = writer(ri, j, dst)
                stage.release("dg", dgv)

    prep_transpose_cast(inproj_in, 8512, 256,
                        lambda ri, j, stage: k.dma("sync", "dg",
                            blob_w("in_projT", 128 * j, 128, 128 * ri,
                                   min(128, 8512 - 128 * ri), 8512), stage))

    k.dma("sync", "dl", sb["small"][:, 8:12],
          AP(normw_in.ap().tensor, 0, [[1, 128], [128, 4]]), allow_slow_non_contiguous=True)
    k.wait("scalar", "dl")
    prep_transpose_cast(outproj_in, 2048, 512,
                        lambda ri, j, stage: k.dma("sync", "dg",
                            blob_w("out_projT", 128 * j, 128, 128 * ri, 128, 2048), stage),
                        scale_col=lambda j: sb["small"][:, 8 + j:9 + j])

    prep_transpose_cast(weight_in, 2048, 256,
                        lambda ri, j, stage: k.dma("sync", "dg",
                            blob_w("weightT", 128 * j, 128, 128 * ri, 128, 2048), stage),
                        scale_col=lambda j: 0.5)

    k.dma("sync", "dl", sb["small"][:, 16:32],
          AP(gnorm_in.ap().tensor, 0, [[1, 128], [128, 16]]), allow_slow_non_contiguous=True)
    k.wait("scalar", "dl")

    def prep_rowscale_cast(src_dram, n_r, n_c, name_or_writer, width, gcol_off):
        for ri in range(n_r // 128):
            wbf = sb["prep"][:, 0:n_c]
            k.wait("gpsimd", "ss")
            k.dma("gpsimd", "dg", wbf, src_dram[128 * ri:128 * (ri + 1), 0:n_c])
            k.wait("scalar", "dg")
            stage = stg[ri % 2]
            stage.acquire("scalar")
            dst = stage.ap[:, 0:n_c] if n_c <= 64 else sb["prepc"][:, 0:n_c]
            if n_c > 64:
                k.wait("scalar", "dg")
            if gcol_off is not None:
                k.do("scalar", "activation", out=dst, in_=wbf, func=AF.Copy,
                     scale=sb["small"][:, gcol_off + ri:gcol_off + ri + 1])
            else:
                k.do("scalar", "copy", out=dst, in_=wbf)
            ssv = k.sig("scalar", "ss")
            k.wait("sync", "ss", ssv)
            if callable(name_or_writer):
                dgv = name_or_writer(ri, dst)
            else:
                dgv = k.dma("sync", "dg", blob_w(name_or_writer, 128 * ri, 128, 0, n_c, width), dst)
            if n_c <= 64:
                stage.release("dg", dgv)

    prep_rowscale_cast(wq_in, 2048, 256, "wq", 256, 16)
    prep_rowscale_cast(wout_in, 2048, 256, "wout", 256, None)
    prep_rowscale_cast(wkv_in, 2048, 256,
                       lambda ri, dst: k.dma("sync", "dg",
                           AP(wblob_in.ap().tensor,
                              SHARD_OFF["small"] + OFF_WKV + 128 * ri * 256,
                              [[256, 128], [1, 256]]), dst),
                       None, 16)

    wb = sb["prepb"][:, 0:2048]
    k.wait("gpsimd", "st")
    k.dma("gpsimd", "dg", wb, aw_in[0:128, :])
    k.wait("tensor", "dg")
    tp_evac_store(wb, 128,
                  lambda j, stage: k.dma("sync", "dg",
                      AP(wblob_in.ap().tensor, SHARD_OFF["small"] + OFF_AW + 128 * j * 128,
                         [[128, 128], [1, 128]]), stage), 16)
    k.wait("gpsimd", "st")
    k.dma("gpsimd", "dg", wb[0:8, :], gatew_in[0:8, :])
    k.wait("tensor", "dg")
    tp_evac_store(wb[0:8, :], 8,
                  lambda j, stage: k.dma("sync", "dg",
                      AP(wblob_in.ap().tensor, SHARD_OFF["small"] + 128 * j * 8,
                         [[8, 128], [1, 8]]), stage), 16)
    for e in range(8):
        wbf = sb["prepb"][:, 0:256]
        k.wait("gpsimd", "st")
        for oc in range(16):
            k.dma("gpsimd", "dg", wbf[:, 16 * oc:16 * (oc + 1)],
                  bw_in[2048 * e + 128 * oc:2048 * e + 128 * (oc + 1), :])
        k.wait("tensor", "dg")
        for oc in range(16):
            bank = pbk[oc % 2]
            stage = stg[oc % 2]
            bank.acquire("tensor")
            stv = k.doi("tensor", "st", "transpose", bank.ap[0:16, 0:128],
                        wbf[:, 16 * oc:16 * (oc + 1)], sb["idn"][:, 0:128])
            k.wait("scalar", "st", stv)
            stage.acquire("scalar")
            dst = stage.ap[0:16, 0:128]
            k.do("scalar", "copy", out=dst, in_=bank.ap[0:16, 0:128])
            ssv = k.sig("scalar", "ss")
            bank.release("ss", ssv)
            k.wait("sync", "ss", ssv)
            dgv = k.dma("sync", "dg",
                        AP(wblob_in.ap().tensor,
                           SHARD_OFF["small"] + OFF_BW + 16 * e * 2048 + 128 * oc,
                           [[2048, 16], [1, 128]]), dst)
            stage.release("dg", dgv)

    k.wait("gpsimd", "dg")
    k.wait("gpsimd", "ss")
    k.wait("gpsimd", "sv")
    CC_W = k.cc("AllGather", ALU.bypass, [[0, 1, 2, 3, 4, 5, 6, 7]],
                [wblob_in[:].opt()], [wblob[:].opt()])
    k.wait("sync", "scc", CC_W)

    def layernorm_to_xnf():
        xnt = sb["gbuf"]
        for i in range(NT):
            stc = sb["stat"][:, 0:12]
            bnst = sb["stat"][:, 24:48]
            for q4 in range(4):
                k.do("vector", "bn_stats", out=bnst[:, 6 * q4:6 * (q4 + 1)],
                     in_=xt(i)[:, 512 * q4:512 * (q4 + 1)])
            k.drain("vector")
            k.do("vector", "bn_aggr", out=stc[:, 6:8], in_=bnst)
            k.drain("vector")
            k.do("vector", "tensor_scalar_add", out=stc[:, 8:9],
                 in0=stc[:, 7:8], scalar1=EPS_LN)
            k.drain("vector")
            k.do("vector", "reciprocal", out=stc[:, 9:10], in_=stc[:, 8:9])
            svv = k.sig("vector", "sv")
            k.wait("scalar", "sv", svv)
            k.do("scalar", "activation", out=stc[:, 10:11], in_=stc[:, 9:10], func=AF.Sqrt)
            ssv = k.sig("scalar", "ss")
            k.wait("vector", "ss", ssv)
            k.do("vector", "tensor_scalar", out=stc[:, 11:12], in0=stc[:, 6:7],
                 scalar1=-1.0, scalar2=None, op0=ALU.mult)
            k.drain("vector")
            k.do("vector", "tensor_scalar", out=xnt[:, 2048 * i:2048 * (i + 1)], in0=xt(i),
                 scalar1=stc[:, 11:12], scalar2=stc[:, 10:11], op0=ALU.add, op1=ALU.mult)
        svv = k.sig("vector", "sv")
        k.wait("tensor", "sv", svv)
        for i in range(NT):
            for c in range(ND):
                bank = pbk[c % 2]
                bank.acquire("tensor")
                stv = k.doi("tensor", "st", "transpose", bank.ap[:, 0:128],
                            xnt[:, 2048 * i + 128 * c:2048 * i + 128 * (c + 1)],
                            sb["idn"][:, 0:128])
                eng, sem = (("vector", "sv") if c % 2 == 0 else ("scalar", "ss"))
                k.wait(eng, "st", stv)
                k.do(eng, "tensor_copy" if eng == "vector" else "copy",
                     out=xnfc(c, 128 * i, 128), in_=bank.ap[:, 0:128])
                ev = k.sig(eng, sem)
                bank.release(sem, ev)

    def xnf_barrier():
        k.wait("tensor", "sv")
        k.wait("tensor", "ss")

    def sub_ffn0():
        layernorm_to_xnf()
        xnf_barrier()
        gfm = sb["gbuf"]
        for half in range(2):
            th0 = 512 * half
            stat_bank = pf[5]
            stat_bank.acquire("tensor")
            for wc in range(32):
                buf = sb["wbuf"][:, 4096 * (wc % 2):4096 * (wc % 2) + 4096]
                k.wait("sync", "st")
                rank = (wc * 256) // 1024
                c0 = (wc * 256) % 1024
                for dc in range(16):
                    k.dma("sync", "dw", buf[:, 256 * dc:256 * (dc + 1)],
                          blob_r(rank, "w_fcT", 128 * dc, 128, c0, 256, 1024))
                DW = k.ct["dw"]
                k.wait("tensor", "dw", DW)
                for h2 in range(2):
                    hc = wc * 2 + h2
                    bank = pf[hc % 2]
                    bank.acquire("tensor")
                    for dc in range(16):
                        k.doi("tensor", "st", "matmul", bank.ap[:, 0:512],
                              buf[:, 256 * dc + 128 * h2:256 * dc + 128 * (h2 + 1)],
                              xnfc(dc, th0, 512), start=(dc == 0), stop=(dc == 15))
                    STv = k.ct["st"]
                    rl = sb["prepb"][:, 1024 + 512 * (hc % 2):1024 + 512 * (hc % 2) + 512]
                    g2 = sb["prepb"][:, 512 * (hc % 2):512 * (hc % 2) + 512]
                    k.wait("scalar", "st", STv)
                    k.do("scalar", "activation", out=rl, in_=bank.ap[:, 0:512], func=AF.Relu)
                    ssv = k.sig("scalar", "ss")
                    bank.release("ss", ssv)
                    k.wait("vector", "ss", ssv)
                    k.do("vector", "tensor_tensor", out=gfm[:, 512 * hc:512 * (hc + 1)],
                         in0=rl, in1=rl, op=ALU.mult)
                    k.drain("vector")
                    k.do("vector", "tensor_tensor", out=g2,
                         in0=gfm[:, 512 * hc:512 * (hc + 1)],
                         in1=gfm[:, 512 * hc:512 * (hc + 1)], op=ALU.mult)
                    svv = k.sig("vector", "sv")
                    k.wait("tensor", "sv", svv)
                    k.doi("tensor", "st", "matmul", stat_bank.ap[0:1, 0:512],
                          sb["idn"][:, 128:129], g2, start=(hc == 0), stop=(hc == 63))
            STs = k.ct["st"]
            k.wait("vector", "st", STs)
            k.wait("vector", "dg")
            k.do("vector", "tensor_scalar", out=sb["srow"][0:1, 0:512],
                 in0=stat_bank.ap[0:1, 0:512], scalar1=1.0 / HID0, scalar2=EPS_RMS0,
                 op0=ALU.mult, op1=ALU.add)
            k.drain("vector")
            k.do("vector", "reciprocal", out=sb["srow"][0:1, 0:512],
                 in_=sb["srow"][0:1, 0:512])
            svv = k.sig("vector", "sv")
            stat_bank.release("sv", svv)
            k.wait("scalar", "sv", svv)
            k.do("scalar", "activation", out=sb["srow"][0:1, 512:1024],
                 in_=sb["srow"][0:1, 0:512], func=AF.Sqrt, scale=0.25)
            ssv = k.sig("scalar", "ss")
            k.wait("sync", "ss", ssv)
            k.dma("sync", "dg", dscr_ap(0, [[512, 1], [1, 512]]), sb["srow"][0:1, 512:1024])
            k.wait("sync", "dg")
            RB = k.dma("sync", "dl", sb["rbc"][:, 0:512], dscr_ap(0, [[0, 128], [1, 512]]))
            for dchunk in range(16):
                k.wait("sync", "st")
                for r8 in range(8):
                    base = r8 * SHARD_N + SHARD_OFF["w_projT"] + dchunk * (1024 * 128)
                    k.dma("sync", "dw",
                          sb["wbuf"][:, 1024 * r8:1024 * (r8 + 1)]
                          .rearrange("p (a d) -> p a d", d=128),
                          AP(wblob.ap().tensor, base, [[128, 128], [16384, 8], [1, 128]]))
                DW = k.ct["dw"]
                k.wait("tensor", "dw", DW)
                bank = pf[2 + dchunk % 2]
                bank.acquire("tensor")
                for hc in range(64):
                    r8, a = hc // 8, hc % 8
                    k.doi("tensor", "st", "matmul", bank.ap[:, 0:512],
                          sb["wbuf"][:, 1024 * r8 + 128 * a:1024 * r8 + 128 * (a + 1)],
                          gfm[:, 512 * hc:512 * (hc + 1)],
                          start=(hc == 0), stop=(hc == 63))
                STv = k.ct["st"]
                k.wait("vector", "st", STv)
                if dchunk == 0:
                    k.wait("vector", "dl", RB)
                tf = sb["prep"][:, 512 * (dchunk % 2):512 * (dchunk % 2) + 512]
                k.do("vector", "tensor_tensor", out=tf, in0=bank.ap[:, 0:512],
                     in1=sb["rbc"][:, 0:512], op=ALU.mult)
                svv = k.sig("vector", "sv")
                bank.release("sv", svv)
                k.wait("tensor", "sv", svv)
                obank = pf[4]
                obank.acquire("tensor")
                stv = 0
                for m in range(4):
                    stv = k.doi("tensor", "st", "transpose",
                                obank.ap[:, 128 * m:128 * (m + 1)],
                                tf[:, 128 * m:128 * (m + 1)], sb["idnf"][:])
                k.wait("vector", "st", stv)
                for m in range(4):
                    ti = half * 4 + m
                    k.do("vector", "tensor_tensor",
                         out=xt(ti)[:, 128 * dchunk:128 * (dchunk + 1)],
                         in0=xt(ti)[:, 128 * dchunk:128 * (dchunk + 1)],
                         in1=obank.ap[:, 128 * m:128 * (m + 1)], op=ALU.add)
                svv = k.sig("vector", "sv")
                obank.release("sv", svv)


    def sub_attn0():
        layernorm_to_xnf()
        xnf_barrier()
        g = sb["gbuf"]
        for hc8 in range(8):
            buf = sb["wbuf"][:, 4096 * (hc8 % 2):4096 * (hc8 % 2) + 4096]
            k.wait("sync", "st")
            k.dma("sync", "dw", buf[:, 0:4096].rearrange("p (a c) -> p a c", c=256),
                  AP(wblob.ap().tensor, hc8 * SHARD_N + SHARD_OFF["wq"],
                     [[256, 128], [32768, 16], [1, 256]]))
            DW = k.ct["dw"]
            k.wait("tensor", "dw", DW)
            for ti in range(NT):
                bank = pf[ti % 2]
                bank.acquire("tensor")
                for dc in range(16):
                    k.doi("tensor", "st", "matmul", bank.ap[:, 0:256],
                          xnfc(dc, 128 * ti, 128), buf[:, 256 * dc:256 * (dc + 1)],
                          start=(dc == 0), stop=(dc == 15))
                STv = k.ct["st"]
                eng, sem, meth = (("vector", "sv", "tensor_copy") if ti % 2 == 0
                                  else ("scalar", "ss", "copy"))
                k.wait(eng, "st", STv)
                k.do(eng, meth, out=g[:, 2048 * ti + 256 * hc8:2048 * ti + 256 * (hc8 + 1)],
                     in_=bank.ap[:, 0:256])
                ev = k.sig(eng, sem)
                bank.release(sem, ev)
        buf = sb["wbuf"][:, 0:4096]
        k.wait("sync", "st")
        k.wait("sync", "ss")
        k.wait("sync", "sv")
        k.dma("sync", "dw", buf.rearrange("p (a c) -> p a c", c=256),
              AP(wblob.ap().tensor, 1 * SHARD_N + SHARD_OFF["small"] + OFF_WKV,
                 [[256, 128], [32768, 16], [1, 256]]))
        DWkv = k.ct["dw"]
        k.wait("tensor", "dw", DWkv)
        kvt = sb["prepc"]
        for ti in range(NT):
            bank = pf[2 + ti % 2]
            bank.acquire("tensor")
            for dc in range(16):
                k.doi("tensor", "st", "matmul", bank.ap[:, 0:256],
                      xnfc(dc, 128 * ti, 128), buf[:, 256 * dc:256 * (dc + 1)],
                      start=(dc == 0), stop=(dc == 15))
            STv = k.ct["st"]
            k.wait("scalar", "st", STv)
            k.do("scalar", "activation", out=sb["prep"][:, 0:128], in_=bank.ap[:, 0:128],
                 func=AF.Square, accum_out=sb["stat"][:, 0:1])
            sskv = k.sig("scalar", "ss")
            k.wait("vector", "ss", sskv)
            k.do("vector", "tensor_scalar", out=sb["stat"][:, 1:2], in0=sb["stat"][:, 0:1],
                 scalar1=0.0625, scalar2=6.25e-14, op0=ALU.mult, op1=ALU.add)
            k.drain("vector")
            k.do("vector", "reciprocal", out=sb["stat"][:, 2:3], in_=sb["stat"][:, 1:2])
            svv = k.sig("vector", "sv")
            k.wait("scalar", "sv", svv)
            k.do("scalar", "activation", out=sb["stat"][:, 3:4], in_=sb["stat"][:, 2:3],
                 func=AF.Sqrt)
            ssv = k.sig("scalar", "ss")
            k.wait("vector", "ss", ssv)
            k.do("vector", "tensor_scalar", out=kvt[:, 128 * ti:128 * (ti + 1)],
                 in0=bank.ap[:, 0:128], scalar1=sb["stat"][:, 3:4], scalar2=None,
                 op0=ALU.mult)
            k.drain("vector")
            k.do("vector", "tensor_copy", out=kvt[:, 1024 + 128 * ti:1024 + 128 * (ti + 1)],
                 in_=bank.ap[:, 128:256])
            svv = k.sig("vector", "sv")
            bank.release("sv", svv)
        k.wait("tensor", "sv")
        for ti in range(NT):
            bank = pbk[ti % 2]
            bank.acquire("tensor")
            stv = k.doi("tensor", "st", "transpose", bank.ap[:, 0:128],
                        kvt[:, 128 * ti:128 * (ti + 1)], sb["idn"][:, 0:128])
            k.wait("scalar", "st", stv)
            k.do("scalar", "copy", out=sb["prepb"][:, 128 * ti:128 * (ti + 1)],
                 in_=bank.ap[:, 0:128])
            ev = k.sig("scalar", "ss")
            bank.release("ss", ev)
        k.wait("sync", "ss")
        k.wait("sync", "sv")
        k.dma("sync", "dg", AP(ccd_in.ap().tensor, 0, [[1024, 128], [1, 1024]]),
              sb["prepb"][:, 0:1024])
        k.dma("sync", "dg", AP(ccd_in.ap().tensor, 131072, [[128, 1024], [1, 128]])
              .rearrange("(a p) c -> p a c", p=128),
              kvt[:, 1024:2048].rearrange("p (a c) -> p a c", c=128))
        k.wait("gpsimd", "dg")
        CCKV = k.cc("AllGather", ALU.bypass, [[0, 1], [2, 3], [4, 5], [6, 7]],
                    [ccd_in[:].opt()], [ccd_out[:, :].opt()])
        for ti in range(NT):
            qrow = g[:, 2048 * ti:2048 * (ti + 1)]
            k.wait("vector", "ss")
            sqf = sb["prep"][:, 0:1024]
            k.do("vector", "tensor_tensor", out=sqf, in0=qrow[:, 0:1024],
                 in1=qrow[:, 0:1024], op=ALU.mult)
            k.drain("vector")
            k.do("vector", "reduce_sum", out=sb["stat"][:, 0:8],
                 in_=sqf.rearrange("p (h c) -> p h c", c=128), axis=AX.X)
            k.drain("vector")
            k.do("vector", "tensor_tensor", out=sqf, in0=qrow[:, 1024:2048],
                 in1=qrow[:, 1024:2048], op=ALU.mult)
            k.drain("vector")
            k.do("vector", "reduce_sum", out=sb["stat"][:, 8:16],
                 in_=sqf.rearrange("p (h c) -> p h c", c=128), axis=AX.X)
            k.drain("vector")
            k.do("vector", "tensor_scalar", out=sb["stat"][:, 16:32],
                 in0=sb["stat"][:, 0:16], scalar1=0.0625, scalar2=6.25e-14,
                 op0=ALU.mult, op1=ALU.add)
            k.drain("vector")
            k.do("vector", "reciprocal", out=sb["stat"][:, 32:48], in_=sb["stat"][:, 16:32])
            svv = k.sig("vector", "sv")
            k.wait("scalar", "sv", svv)
            k.do("scalar", "activation", out=sb["stat"][:, 48:64], in_=sb["stat"][:, 32:48],
                 func=AF.Sqrt)
            ssv = k.sig("scalar", "ss")
            k.wait("vector", "ss", ssv)
            k.do("vector", "tensor_tensor", out=qrow, in0=qrow,
                 in1=sb["stat"][:, 48:64].rearrange("p h -> p h ()").broadcast_to([128, 16, 128]),
                 op=ALU.mult)
            svv = k.sig("vector", "sv")
        k.wait("tensor", "sv")
        for ti in range(NT):
            for h in range(16):
                bank = pbk[h % 2]
                bank.acquire("tensor")
                stv = k.doi("tensor", "st", "transpose", bank.ap[:, 0:128],
                            g[:, 2048 * ti + 128 * h:2048 * ti + 128 * (h + 1)],
                            sb["idn"][:, 0:128])
                eng, sem, meth = (("vector", "sv", "tensor_copy") if h % 2 == 0
                                  else ("scalar", "ss", "copy"))
                k.wait(eng, "st", stv)
                k.do(eng, meth, out=g[:, 16384 + 1024 * h + 128 * ti:16384 + 1024 * h + 128 * (ti + 1)],
                     in_=bank.ap[:, 0:128])
                ev = k.sig(eng, sem)
                bank.release(sem, ev)
        k.wait("sync", "sv")
        k.wait("sync", "ss")
        k.wait("sync", "scc", CCKV)
        for slot in range(2):
            k.dma("sync", "dl", sb["prepb"][:, 1024 * slot:1024 * (slot + 1)],
                  AP(ccd_out.ap().tensor, slot * 262144, [[1024, 128], [1, 1024]]))
            k.dma("sync", "dl",
                  kvt[:, 1024 * slot:1024 * (slot + 1)]
                  .rearrange("p (a c) -> p a c", c=128),
                  AP(ccd_out.ap().tensor, slot * 262144 + 131072,
                     [[128, 1024], [1, 128]]).rearrange("(a p) c -> p a c", p=128))
        KVL = k.wait("tensor", "dl")
        for h in range(16):
            qf = g[:, 16384 + 1024 * h:16384 + 1024 * (h + 1)]
            att = qf
            for qc in range(2):
                for kc in range(16):
                    bank = pf[kc % 2]
                    bank.acquire("tensor")
                    stv = k.doi("tensor", "st", "matmul", bank.ap[:, 0:512],
                                sb["prepb"][:, 128 * kc:128 * (kc + 1)],
                                qf[:, 512 * qc:512 * (qc + 1)], start=True, stop=True)
                    k.wait("scalar", "st", stv)
                    k.do("scalar", "activation",
                         out=g[:, 1024 * kc + 512 * qc:1024 * kc + 512 * (qc + 1)],
                         in_=bank.ap[:, 0:512], func=AF.Exp, bias=sb["small"][:, 1:2])
                    ev = k.sig("scalar", "ss")
                    bank.release("ss", ev)
            SSE = k.wait("tensor", "ss")
            for qc in range(2):
                avb = pf[2 + qc]
                avb.acquire("tensor")
                dnb = pf[4 + qc]
                dnb.acquire("tensor")
                for kc in range(16):
                    E = g[:, 1024 * kc + 512 * qc:1024 * kc + 512 * (qc + 1)]
                    k.doi("tensor", "st", "matmul", avb.ap[:, 0:512],
                          kvt[:, 128 * kc:128 * (kc + 1)], E,
                          start=(kc == 0), stop=(kc == 15))
                    k.doi("tensor", "st", "matmul", dnb.ap[0:1, 0:512],
                          sb["idn"][:, 128:129], E, start=(kc == 0), stop=(kc == 15))
            STav = k.ct["st"]
            k.wait("vector", "st", STav)
            k.do("vector", "reciprocal", out=sb["srow"][0:1, 0:512], in_=pf[4].ap[0:1, 0:512])
            k.drain("vector")
            k.do("vector", "reciprocal", out=sb["srow"][0:1, 512:1024], in_=pf[5].ap[0:1, 0:512])
            svv = k.sig("vector", "sv")
            pf[4].release("sv", svv)
            pf[5].release("sv", svv)
            k.wait("sync", "sv", svv)
            k.dma("sync", "dg", dscr_ap(2048 * (h % 2), [[1024, 1], [1, 1024]]),
                  sb["srow"][0:1, 0:1024])
            k.wait("sync", "dg")
            RBh = k.dma("sync", "dl", sb["rbc"][:, 0:1024],
                        dscr_ap(2048 * (h % 2), [[0, 128], [1, 1024]]))
            k.wait("vector", "dl", RBh)
            for qc in range(2):
                k.do("vector", "tensor_tensor", out=att[:, 512 * qc:512 * (qc + 1)],
                     in0=pf[2 + qc].ap[:, 0:512], in1=sb["rbc"][:, 512 * qc:512 * (qc + 1)],
                     op=ALU.mult)
            svv = k.sig("vector", "sv")
            pf[2].release("sv", svv)
            pf[3].release("sv", svv)
            k.wait("tensor", "sv", svv)
        for tp2 in range(4):
            for dc2 in range(4):
                buf = sb["wbuf"][:, 0:8192]
                k.wait("sync", "st")
                for rr in range(2):
                    k.dma("sync", "dw",
                          buf[:, 4096 * rr:4096 * (rr + 1)].rearrange("p (a c) -> p a c", c=256),
                          AP(wblob.ap().tensor,
                             (2 * dc2 + rr) * SHARD_N + SHARD_OFF["wout"],
                             [[256, 128], [32768, 16], [1, 256]]))
                DW = k.ct["dw"]
                k.wait("tensor", "dw", DW)
                for tt in range(2):
                    ti = 2 * tp2 + tt
                    bank = pf[tt]
                    bank.acquire("tensor")
                    for rr in range(2):
                        for hc in range(16):
                            k.doi("tensor", "st", "matmul", bank.ap[:, 256 * rr:256 * (rr + 1)],
                                  g[:, 16384 + 1024 * hc + 128 * ti:16384 + 1024 * hc + 128 * (ti + 1)],
                                  buf[:, 4096 * rr + 256 * hc:4096 * rr + 256 * (hc + 1)],
                                  start=(hc == 0), stop=(hc == 15))
                    STv = k.ct["st"]
                    eng, sem, meth = (("vector", "sv", "tensor_copy") if tt == 0
                                      else ("scalar", "ss", "copy"))
                    k.wait(eng, "st", STv)
                    k.do(eng, meth,
                         out=g[:, 2048 * ti + 512 * dc2:2048 * ti + 512 * (dc2 + 1)],
                         in_=bank.ap[:, 0:512])
                    ev = k.sig(eng, sem)
                    bank.release(sem, ev)
        k.wait("vector", "sv")
        k.wait("vector", "ss")
        k.wait("gpsimd", "st")
        GOB = k.dma("gpsimd", "dl", g[:, 16384:18432],
                    gout_in[0:1, :].partition_broadcast(128))
        k.wait("vector", "dl", GOB)
        for i in range(NT):
            ot = g[:, 2048 * i:2048 * (i + 1)]
            stc = sb["stat"][:, 0:12]
            bnst = sb["stat"][:, 24:48]
            for q4 in range(4):
                k.do("vector", "bn_stats", out=bnst[:, 6 * q4:6 * (q4 + 1)],
                     in_=ot[:, 512 * q4:512 * (q4 + 1)])
            k.drain("vector")
            k.do("vector", "bn_aggr", out=stc[:, 6:8], in_=bnst)
            k.drain("vector")
            k.do("vector", "tensor_scalar_add", out=stc[:, 8:9], in0=stc[:, 7:8],
                 scalar1=EPS_LN)
            k.drain("vector")
            k.do("vector", "reciprocal", out=stc[:, 9:10], in_=stc[:, 8:9])
            svv = k.sig("vector", "sv")
            k.wait("scalar", "sv", svv)
            k.do("scalar", "activation", out=stc[:, 10:11], in_=stc[:, 9:10], func=AF.Sqrt)
            ssv = k.sig("scalar", "ss")
            k.wait("vector", "ss", ssv)
            k.do("vector", "tensor_scalar", out=stc[:, 11:12], in0=stc[:, 6:7],
                 scalar1=-1.0, scalar2=None, op0=ALU.mult)
            k.drain("vector")
            tno = sb["prep"][:, 0:1024]
            for hq in range(2):
                k.do("vector", "tensor_scalar", out=tno,
                     in0=ot[:, 1024 * hq:1024 * (hq + 1)],
                     scalar1=stc[:, 11:12], scalar2=stc[:, 10:11],
                     op0=ALU.add, op1=ALU.mult)
                k.drain("vector")
                k.do("vector", "tensor_tensor", out=tno, in0=tno,
                     in1=g[:, 16384 + 1024 * hq:16384 + 1024 * (hq + 1)], op=ALU.mult)
                k.drain("vector")
                k.do("vector", "tensor_tensor", out=xt(i)[:, 1024 * hq:1024 * (hq + 1)],
                     in0=xt(i)[:, 1024 * hq:1024 * (hq + 1)], in1=tno, op=ALU.add)
                k.drain("vector")
        k.sig("vector", "sv")

    def sub_ffn1():
        layernorm_to_xnf()
        xnf_barrier()
        g = sb["gbuf"]
        k.wait("sync", "sv")
        k.wait("sync", "ss")
        for dc in range(16):
            k.dma("sync", "dw", g[:, 136 * dc:136 * dc + 128],
                  AP(wblob.ap().tensor, 0 * SHARD_N + SHARD_OFF["small"] + OFF_AW + 128 * dc * 128,
                     [[128, 128], [1, 128]]))
            k.dma("sync", "dw", g[:, 136 * dc + 128:136 * (dc + 1)],
                  AP(wblob.ap().tensor, 1 * SHARD_N + SHARD_OFF["small"] + 1024 * dc,
                     [[8, 128], [1, 8]]))
        k.dma("sync", "dw", g[:, 4096:6144],
              AP(wblob.ap().tensor, 2 * SHARD_N + SHARD_OFF["small"] + OFF_BW,
                 [[2048, 128], [1, 2048]]))
        DWA = k.ct["dw"]
        k.wait("tensor", "dw", DWA)
        for ti in range(NT):
            bank = pf[ti % 2]
            bank.acquire("tensor")
            for dc in range(16):
                k.doi("tensor", "st", "matmul", bank.ap[:, 0:136],
                      xnfc(dc, 128 * ti, 128), g[:, 136 * dc:136 * (dc + 1)],
                      start=(dc == 0), stop=(dc == 15))
            STv = k.ct["st"]
            st = sb["stat"]
            k.wait("vector", "st", STv)
            k.do("vector", "reduce_max", out=st[:, 0:1], in_=bank.ap[:, 128:136], axis=AX.X)
            k.drain("vector")
            k.do("vector", "tensor_scalar", out=st[:, 1:2], in0=st[:, 0:1],
                 scalar1=-1.0, scalar2=None, op0=ALU.mult)
            svv = k.sig("vector", "sv")
            k.wait("scalar", "sv", svv)
            k.do("scalar", "activation", out=st[:, 8:16], in_=bank.ap[:, 128:136],
                 func=AF.Exp, bias=st[:, 1:2])
            ssv = k.sig("scalar", "ss")
            k.wait("vector", "ss", ssv)
            k.do("vector", "tensor_scalar", out=st[:, 16:24], in0=st[:, 8:16],
                 scalar1=1.0, scalar2=None, op0=ALU.is_ge)
            k.drain("vector")
            k.do("vector", "tensor_tensor", out=st[:, 24:32], in0=st[:, 8:16],
                 in1=st[:, 16:24], op=ALU.subtract)
            k.drain("vector")
            k.do("vector", "reduce_max", out=st[:, 2:3], in_=st[:, 24:32], axis=AX.X)
            k.drain("vector")
            k.do("vector", "tensor_scalar", out=st[:, 16:24], in0=st[:, 8:16],
                 scalar1=st[:, 2:3], scalar2=None, op0=ALU.is_ge)
            k.drain("vector")
            k.do("vector", "tensor_scalar_add", out=st[:, 3:4], in0=st[:, 2:3], scalar1=1.0)
            k.drain("vector")
            k.do("vector", "reciprocal", out=st[:, 4:5], in_=st[:, 3:4])
            k.drain("vector")
            k.do("vector", "tensor_tensor", out=st[:, 24:32], in0=st[:, 8:16],
                 in1=st[:, 16:24], op=ALU.mult)
            k.drain("vector")
            k.do("vector", "tensor_scalar", out=st[:, 8:16], in0=st[:, 24:32],
                 scalar1=st[:, 4:5], scalar2=8.0, op0=ALU.mult, op1=ALU.mult)
            k.drain("vector")
            tsc = sb["prepb"][:, 128 * (ti % 2):128 * (ti % 2) + 128]
            k.do("vector", "tensor_tensor", out=tsc, in0=bank.ap[:, 0:128],
                 in1=st[:, 8:16].rearrange("p e -> p e ()").broadcast_to([128, 8, 16]),
                 op=ALU.mult)
            svv = k.sig("vector", "sv")
            bank.release("sv", svv)
            k.wait("tensor", "sv", svv)
            tb = pbk[ti % 2]
            tb.acquire("tensor")
            stv = k.doi("tensor", "st", "transpose", tb.ap[:, 0:128], tsc, sb["idn"][:, 0:128])
            k.wait("scalar", "st", stv)
            k.do("scalar", "copy", out=g[:, 6144 + 128 * ti:6144 + 128 * (ti + 1)],
                 in_=tb.ap[:, 0:128])
            ev = k.sig("scalar", "ss")
            tb.release("ss", ev)
        k.wait("tensor", "ss")
        for oc in range(4):
            k.wait("sync", "st")
            for hb in range(16):
                r, a = hb // 2, hb % 2
                k.dma("sync", "dw", sb["wbuf"][:, 512 * hb:512 * (hb + 1)],
                      AP(wblob.ap().tensor,
                         r * SHARD_N + SHARD_OFF["weightT"] + 128 * a * 2048 + 512 * oc,
                         [[2048, 128], [1, 512]]))
            DW = k.ct["dw"]
            k.wait("tensor", "dw", DW)
            for ti in range(NT):
                bank = pf[2 + ti % 2]
                bank.acquire("tensor")
                for hb in range(16):
                    k.doi("tensor", "st", "matmul", bank.ap[:, 0:512],
                          xnfc(hb, 128 * ti, 128), sb["wbuf"][:, 512 * hb:512 * (hb + 1)],
                          start=(hb == 0), stop=False)
                k.doi("tensor", "st", "matmul", bank.ap[:, 0:512],
                      g[:, 6144 + 128 * ti:6144 + 128 * (ti + 1)],
                      g[:, 4096 + 512 * oc:4096 + 512 * (oc + 1)],
                      start=False, stop=True)
                STv = k.ct["st"]
                k.wait("vector", "st", STv)
                k.do("vector", "tensor_tensor", out=xt(ti)[:, 512 * oc:512 * (oc + 1)],
                     in0=xt(ti)[:, 512 * oc:512 * (oc + 1)], in1=bank.ap[:, 0:512],
                     op=ALU.add)
                svv = k.sig("vector", "sv")
                bank.release("sv", svv)
        k.sig("vector", "sv")

    ACS_D, EA_D, CD_D, RST_D, DPAR_D = 0, 65536, 131072, 131328, 132352
    IPT = SHARD_OFF["in_projT"]
    OPT = SHARD_OFF["out_projT"]

    def sub_mamba():
        layernorm_to_xnf()
        xnf_barrier()
        g = sb["gbuf"]
        mam, ystg, wb = sb["mam"], sb["ystg"], sb["wbuf"]
        sm = sb["small"]
        k.wait("vector", "st")
        k.do("vector", "memset", wb[:, 6656:6784], 0.0)
        k.do("vector", "memset", wb[:, 6784:7040], 1.0)
        svm = k.sig("vector", "sv")
        k.wait("gpsimd", "sv", svm)
        k.doi("gpsimd", "sgp", "affine_select", out=wb[:, 6784:6912], in_=wb[:, 6784:6912],
              pattern=[[1, 128]], compare_op=ALU.is_ge, fill=0.0,
              base=0, channel_multiplier=-1)
        SGPM = k.ct["sgp"]
        k.dma("sync", "dl", sm[0:64, 2:3], alog_in[0:64, 0:1])
        k.dma("sync", "dl", sm[0:64, 3:4], dtb_in[0:64, 0:1])
        k.dma("sync", "dl", sm[0:64, 6:7], dpar_in[0:64, 0:1])
        DL0 = k.ct["dl"]
        k.wait("scalar", "dl", DL0)
        k.do("scalar", "activation", out=sm[0:64, 4:5], in_=sm[0:64, 2:3], func=AF.Exp)
        ssv = k.sig("scalar", "ss")
        k.wait("vector", "ss", ssv)
        k.do("vector", "tensor_scalar", out=sm[0:64, 5:6], in0=sm[0:64, 4:5],
             scalar1=-1.0, scalar2=None, op0=ALU.mult)
        k.sig("vector", "sv")
        k.wait("sync", "sv")
        k.dma("sync", "dg", dscr_ap(DPAR_D, [[1, 64], [1, 1]]), sm[0:64, 6:7])
        k.wait("sync", "dg")
        DPB = k.dma("sync", "dl", sm[:, 32:96], dscr_ap(DPAR_D, [[0, 128], [1, 64]]))

        for j in range(35):
            nch = 128 if j < 34 else 64
            wslot = sb["wbuf"][:, 2048 * (j % 2):2048 * (j % 2) + 2048]
            k.wait("sync", "st")
            for dc in range(16):
                r, a = dc // 2, dc % 2
                k.dma("sync", "dw", wslot[:, 128 * dc:128 * dc + nch],
                      AP(wblob.ap().tensor,
                         r * SHARD_N + IPT + 8512 * 128 * a + 4096 + 128 * j,
                         [[8512, 128], [1, nch]]))
            DW = k.ct["dw"]
            k.wait("tensor", "dw", DW)
            for th in range(2):
                bank = pf[th]
                bank.acquire("tensor")
                for dc in range(16):
                    k.doi("tensor", "st", "matmul", bank.ap[0:nch, 0:512],
                          wslot[:, 128 * dc:128 * dc + nch], xnfc(dc, 512 * th, 512),
                          start=(dc == 0), stop=(dc == 15))
                STv = k.ct["st"]
                if j < 32:
                    dst = g[:, 1024 * j + 512 * th:1024 * j + 512 * (th + 1)]
                elif j < 34:
                    jj = j - 32
                    dst = sb["prepb"][:, 1024 * jj + 512 * th:1024 * jj + 512 * (th + 1)]
                else:
                    dst = sb["prep"][0:64, 512 * th:512 * (th + 1)]
                eng, sem, meth = (("scalar", "ss", "copy") if th == 0
                                  else ("vector", "sv", "tensor_copy"))
                k.wait(eng, "st", STv)
                k.do(eng, meth, out=dst,
                     in_=bank.ap[0:64, 0:512] if j == 34 else bank.ap[:, 0:512])
                ev = k.sig(eng, sem)
                bank.release(sem, ev)
        k.wait("sync", "ss")
        k.wait("sync", "sv")
        for j in range(34):
            src = (g[:, 1024 * j + 1021:1024 * j + 1024] if j < 32 else
                   sb["prepb"][:, 1024 * (j - 32) + 1021:1024 * (j - 32) + 1024])
            k.dma("sync", "dg", AP(ccd_in.ap().tensor, 384 * j, [[3, 128], [1, 3]]), src)
        k.wait("gpsimd", "dg")
        CCH = k.cc("AllGather", ALU.bypass, [[0, 1], [2, 3], [4, 5], [6, 7]],
                   [ccd_in[:].opt()], [ccd_out[:, :].opt()])

        A_ = sb["prep"][0:64, 0:1024]
        B_ = mam[0:64, 0:1024]
        k.wait("scalar", "sv")
        k.wait("vector", "ss")
        for qf in range(4):
            C_ = sb["rbc"][0:64, 0:256]
            raw = A_[:, 256 * qf:256 * (qf + 1)]
            dth = B_[:, 256 * qf:256 * (qf + 1)]
            if qf:
                k.wait("scalar", "sv")
            k.do("scalar", "activation", out=C_, in_=raw, func=AF.Abs, bias=sm[0:64, 3:4])
            k.drain("scalar")
            k.do("scalar", "activation", out=C_, in_=C_, func=AF.Exp, scale=-1.0)
            k.drain("scalar")
            k.do("scalar", "activation", out=C_, in_=C_, func=AF.Ln, bias=1.0)
            k.drain("scalar")
            k.do("scalar", "activation", out=dth, in_=raw, func=AF.Relu, bias=sm[0:64, 3:4])
            ssv = k.sig("scalar", "ss")
            k.wait("vector", "ss", ssv)
            k.do("vector", "tensor_tensor", out=dth, in0=dth, in1=C_, op=ALU.add)
            k.sig("vector", "sv")
        svv = k.sig("vector", "sv")
        k.wait("tensor", "sv", svv)
        for st8 in range(8):
            bank = pf[2 + st8 % 2]
            bank.acquire("tensor")
            stv = k.doi("tensor", "st", "transpose", bank.ap[:, 0:64],
                        B_[:, 128 * st8:128 * (st8 + 1)], sb["idnf"][0:64, 0:64])
            k.wait("scalar", "st", stv)
            k.do("scalar", "copy", out=wb[:, 7168 + 64 * st8:7168 + 64 * (st8 + 1)],
                 in_=bank.ap[:, 0:64])
            ev = k.sig("scalar", "ss")
            bank.release("ss", ev)
        k.wait("vector", "st")
        k.do("vector", "tensor_scalar", out=B_, in0=B_, scalar1=sm[0:64, 5:6],
             scalar2=None, op0=ALU.mult)
        k.sig("vector", "sv")

        k.wait("sync", "scc", CCH)
        for j in range(34):
            k.dma("sync", "dl", wb[:, 7040 + 3 * j:7040 + 3 * j + 3],
                  AP(ccd_out.ap().tensor, 384 * j, [[3, 128], [1, 3]]))
        k.wait("vector", "dl")
        k.do("vector", "tensor_scalar", out=wb[:, 7040:7142], in0=wb[:, 7040:7142],
             scalar1=sm[:, 0:1], scalar2=None, op0=ALU.mult)
        svh = k.sig("vector", "sv")
        k.wait("gpsimd", "sv", svh)
        k.wait("scalar", "sv", svh)
        wres = [Res(k, None), Res(k, None)]
        for j in range(34):
            wsl = sb["stat"][:, 12 + 4 * (j % 2):16 + 4 * (j % 2)]
            wres[j % 2].acquire("gpsimd")
            DLW = k.dma("gpsimd", "dl", wsl, convw_in[128 * j:128 * (j + 1), 0:4])
            raw = (g[:, 1024 * j:1024 * (j + 1)] if j < 32 else
                   sb["prepb"][:, 1024 * (j - 32):1024 * (j - 31)])
            acc = sb["prep"][:, 0:1024]
            k.wait("scalar", "dl", DLW)
            k.do("scalar", "activation", out=acc, in_=raw, func=AF.Copy, scale=wsl[:, 3:4])
            for kk in range(3):
                s = 3 - kk
                tmp = sb["prepc"][:, 0:1024 - s]
                tmp2 = sb["prepc"][:, 1024:1024 + s]
                k.do("scalar", "activation", out=tmp, in_=raw[:, 0:1024 - s],
                     func=AF.Copy, scale=wsl[:, kk:kk + 1])
                k.do("scalar", "activation", out=tmp2,
                     in_=wb[:, 7040 + 3 * j + kk:7040 + 3 * j + 3],
                     func=AF.Copy, scale=wsl[:, kk:kk + 1])
                ssv = k.sig("scalar", "ss")
                k.wait("vector", "ss", ssv)
                k.do("vector", "tensor_tensor", out=acc[:, s:1024], in0=acc[:, s:1024],
                     in1=tmp, op=ALU.add)
                k.drain("vector")
                k.do("vector", "tensor_tensor", out=acc[:, 0:s], in0=acc[:, 0:s],
                     in1=tmp2, op=ALU.add)
                svv = k.sig("vector", "sv")
                k.wait("scalar", "sv", svv)
            k.do("scalar", "activation", out=raw, in_=acc, func=AF.Silu)
            wres[j % 2].release("ss", k.sig("scalar", "ss"))

        k.wait("vector", "ss")
        for c in range(NCH):
            k.do("vector", "tensor_tensor_scan", out=A_[:, 256 * c:256 * (c + 1)],
                 data0=B_[:, 256 * c:256 * (c + 1)], data1=B_[:, 256 * c:256 * (c + 1)],
                 initial=0.0, op0=ALU.add, op1=ALU.bypass)
            k.drain("vector")
        svv = k.sig("vector", "sv")
        k.wait("sync", "sv", svv)
        k.dma("sync", "dg", dscr_ap(ACS_D, [[1024, 64], [1, 1024]]), A_)
        k.wait("scalar", "sv", svv)
        for c in range(NCH):
            Cc = sb["rbc"][0:64, 0:256]
            Dc = mam[0:64, 0:256]
            k.wait("vector", "ss")
            k.wait("vector", "dg")
            k.do("vector", "tensor_scalar", out=Cc, in0=A_[:, 256 * c:256 * (c + 1)],
                 scalar1=A_[:, 256 * c + 255:256 * c + 256], scalar2=None, op0=ALU.subtract)
            svv = k.sig("vector", "sv")
            k.wait("scalar", "sv", svv)
            k.do("scalar", "activation", out=Dc, in_=Cc, func=AF.Exp, scale=-1.0)
            ssv = k.sig("scalar", "ss")
            k.wait("tensor", "ss", ssv)
            for sh in range(2):
                st8 = 2 * c + sh
                bank = pf[2 + sh]
                bank.acquire("tensor")
                stv = k.doi("tensor", "st", "transpose", bank.ap[:, 0:64],
                            Dc[:, 128 * sh:128 * (sh + 1)], sb["idnf"][0:64, 0:64])
                k.wait("vector", "st", stv)
                k.do("vector", "tensor_copy",
                     out=wb[:, 7680 + 64 * st8:7680 + 64 * (st8 + 1)], in_=bank.ap[:, 0:64])
                ev = k.sig("vector", "sv")
                bank.release("sv", ev)
            k.do("scalar", "activation", out=Cc, in_=A_[:, 256 * c:256 * (c + 1)], func=AF.Exp)
            ssv = k.sig("scalar", "ss")
            k.wait("sync", "ss", ssv)
            k.dma("sync", "dg", dscr_ap(EA_D + 256 * c, [[1024, 64], [1, 256]]), Cc)
            k.dma("sync", "dg", dscr_ap(CD_D + c, [[4, 64], [1, 1]]), Cc[:, 255:256],
                  allow_slow_non_contiguous=True)
        k.wait("sync", "dg")
        CDB = k.dma("sync", "dl", sb["rbc"][:, 768:1024], dscr_ap(CD_D, [[0, 128], [1, 256]]))
        k.wait("tensor", "ss")
        for st8 in range(8):
            bank = pbk[st8 % 2]
            bank.acquire("tensor")
            stv = k.doi("tensor", "st", "transpose", bank.ap[:, 0:128],
                        sb["prepb"][:, 128 * st8:128 * (st8 + 1)], sb["idn"][:, 0:128])
            eng, sem, meth = (("vector", "sv", "tensor_copy") if st8 % 2 == 0
                              else ("scalar", "ss", "copy"))
            k.wait(eng, "st", stv)
            k.do(eng, meth, out=sb["prepc"][:, 128 * st8:128 * (st8 + 1)],
                 in_=bank.ap[:, 0:128])
            ev = k.sig(eng, sem)
            bank.release(sem, ev)
        for st8 in range(8):
            bank = pf[2 + st8 % 2]
            bank.acquire("tensor")
            stv = k.doi("tensor", "st", "transpose", bank.ap[:, 0:64],
                        A_[:, 128 * st8:128 * (st8 + 1)], sb["idnf"][0:64, 0:64])
            k.wait("scalar", "st", stv)
            k.do("scalar", "copy", out=mam[:, 64 * st8:64 * (st8 + 1)],
                 in_=bank.ap[:, 0:64])
            ev = k.sig("scalar", "ss")
            bank.release("ss", ev)

        XD, YP, X2, DTT, DET = 4096, 4608, 7040, 7168, 7680
        xp = [Res(k, None), Res(k, None)]
        yp = [Res(k, None), Res(k, None)]

        def build_pair(c, j):
            xs = XD + 256 * (j % 2)
            r = xp[j % 2]
            for sh in range(2):
                bank = pbk[sh]
                bank.acquire("tensor")
                stv = k.doi("tensor", "st", "transpose", bank.ap[:, 0:128],
                            g[:, 1024 * j + 256 * c + 128 * sh:1024 * j + 256 * c + 128 * (sh + 1)],
                            sb["idn"][:, 0:128])
                k.wait("vector", "st", stv)
                if sh == 0:
                    r.acquire("vector")
                k.do("vector", "tensor_tensor",
                     out=wb[:, xs + 128 * sh:xs + 128 * (sh + 1)],
                     in0=bank.ap[:, 0:128],
                     in1=wb[:, DTT + 64 * (2 * c + sh) + 2 * j:DTT + 64 * (2 * c + sh) + 2 * j + 2]
                     .rearrange("p h -> p h ()").broadcast_to([128, 2, 64]),
                     op=ALU.mult)
                ev = k.sig("vector", "sv")
                bank.release("sv", ev)
            return xs, r

        def xdt2_slots(c, h, xs):
            hh = h % 2
            for sh in range(2):
                st8 = 2 * c + sh
                k.do("vector", "tensor_tensor", out=wb[:, X2 + 64 * sh:X2 + 64 * (sh + 1)],
                     in0=wb[:, xs + 128 * sh + 64 * hh:xs + 128 * sh + 64 * hh + 64],
                     in1=wb[:, DET + 64 * st8 + h:DET + 64 * st8 + h + 1]
                     .rearrange("p o -> p o ()").broadcast_to([128, 1, 64]),
                     op=ALU.mult)
                k.drain("vector")

        k.wait("vector", "dl", CDB)
        for c in range(NCH):
            for j in range(32):
                xs, xr = build_pair(c, j)
                for hh in range(2):
                    h = 2 * j + hh
                    xdt2_slots(c, h, xs)
                    svx = k.sig("vector", "sv")
                    if hh == 1:
                        xr.release("sv", svx)
                    k.wait("tensor", "sv", svx)
                    bank = pf[h % 2]
                    bank.acquire("tensor")
                    for sh in range(2):
                        k.doi("tensor", "st", "matmul", bank.ap[:, 0:64],
                              sb["prepc"][:, 128 * (2 * c + sh):128 * (2 * c + sh) + 128],
                              wb[:, X2 + 64 * sh:X2 + 64 * (sh + 1)],
                              start=(sh == 0), stop=(sh == 1))
                    STv = k.ct["st"]
                    k.wait("vector", "st", STv)
                    Ps = ystg[:, 64 * h:64 * (h + 1)]
                    if c == 0:
                        k.do("vector", "tensor_copy", out=Ps, in_=bank.ap[:, 0:64])
                    else:
                        k.do("vector", "tensor_scalar", out=Ps, in0=Ps,
                             scalar1=sb["rbc"][:, 768 + 4 * h + c:768 + 4 * h + c + 1],
                             scalar2=None, op0=ALU.mult)
                        k.drain("vector")
                        k.do("vector", "tensor_tensor", out=Ps, in0=Ps, in1=bank.ap[:, 0:64],
                             op=ALU.add)
                    ev = k.sig("vector", "sv")
                    bank.release("sv", ev)
        k.wait("gpsimd", "sv")
        k.dma("gpsimd", "dg", AP(ccm_in.ap().tensor, 0, [[4096, 128], [1, 4096]]),
              ystg[:, 0:4096])
        k.wait("gpsimd", "dg")
        CCS = k.cc("AllGather", ALU.bypass, [[0, 1], [2, 3], [4, 5], [6, 7]],
                   [ccm_in[:].opt()], [ccm_out[:, :].opt()])
        k.wait("gpsimd", "scc", CCS)
        PIN = k.dma("gpsimd", "dl", ystg[:, 0:4096],
                    AP(ccm_out.ap().tensor, 0, [[4096, 128], [1, 4096]]))
        k.wait("vector", "dl", PIN)
        k.wait("vector", "dl", DPB)
        k.do("vector", "tensor_scalar", out=ystg[:, 0:4096], in0=ystg[:, 0:4096],
             scalar1=sm[:, 0:1], scalar2=None, op0=ALU.mult)
        k.sig("vector", "sv")

        rAcs = [Res(k, sb["rbc"][:, 0:256]), Res(k, sb["rbc"][:, 256:512])]
        rEa = Res(k, sb["rbc"][0:64, 512:768])
        t2s = [Res(k, mam[0:64, 512:768]), Res(k, mam[0:64, 768:1024])]
        k.wait("vector", "sgp", SGPM)
        for c in range(NCH):
            k.wait("vector", "dg")
            bq = pf[1]
            stg4 = [0, 0]
            for j in range(32):
                xs, xr = build_pair(c, j)
                ys = YP + 256 * (j % 2)
                yr = yp[j % 2]
                for hh in range(2):
                    h = 2 * j + hh
                    ra = rAcs[hh]
                    ra.acquire("sync")
                    RA = k.dma("sync", "dl", ra.ap,
                               dscr_ap(ACS_D + 1024 * h + 256 * c, [[0, 128], [1, 256]]))
                    rEa.acquire("sync")
                    RE = k.dma("sync", "dl", rEa.ap,
                               dscr_ap(EA_D + 1024 * h + 256 * c, [[0, 64], [1, 256]]))
                    seg = sb["prep"][:, 576:832]
                    ebuf = sb["prepc"][:, 1536:1792]
                    cb = pf[0]
                    cb.acquire("tensor")
                    for sh in range(2):
                        st8 = 2 * c + sh
                        stv = k.doi("tensor", "st", "matmul", cb.ap[:, 256 * sh:256 * (sh + 1)],
                                    sb["prepb"][:, 256 * c + 128 * sh:256 * c + 128 * (sh + 1)],
                                    sb["prepb"][:, 1024 + 256 * c:1024 + 256 * (c + 1)],
                                    start=True, stop=True)
                        if sh == 0:
                            k.wait("vector", "dl", RA)
                        k.do("vector", "tensor_scalar", out=seg, in0=ra.ap,
                             scalar1=mam[:, 64 * st8 + h:64 * st8 + h + 1],
                             scalar2=0.0, op0=ALU.subtract, op1=ALU.min)
                        svv = k.sig("vector", "sv")
                        k.wait("scalar", "sv", svv)
                        k.do("scalar", "activation", out=ebuf, in_=seg, func=AF.Exp)
                        ssv = k.sig("scalar", "ss")
                        k.wait("vector", "ss", ssv)
                        msk = wb[:, 6784:7040] if sh == 0 else wb[:, 6656:6912]
                        k.do("vector", "tensor_tensor", out=ebuf, in0=ebuf, in1=msk, op=ALU.mult)
                        k.drain("vector")
                        k.wait("vector", "st", stv)
                        k.do("vector", "tensor_tensor",
                             out=sb["prepc"][:, 1024 + 256 * sh:1024 + 256 * (sh + 1)],
                             in0=cb.ap[:, 256 * sh:256 * (sh + 1)], in1=ebuf, op=ALU.mult)
                        ev = k.sig("vector", "sv")
                    xdt2_slots(c, h, xs)
                    svA = k.sig("vector", "sv")
                    ra.release("sv", svA)
                    if hh == 1:
                        xr.release("sv", svA)
                    cb.release("sv", svA)
                    k.wait("tensor", "sv", svA)
                    bd = pf[2]
                    bd.acquire("tensor")
                    for sh in range(2):
                        k.doi("tensor", "st", "matmul", bd.ap[0:64, 0:256],
                              wb[:, xs + 128 * sh + 64 * hh:xs + 128 * sh + 64 * hh + 64],
                              sb["prepc"][:, 1024 + 256 * sh:1024 + 256 * (sh + 1)],
                              start=(sh == 0), stop=(sh == 1))
                    bo = pf[3]
                    bo.acquire("tensor")
                    k.doi("tensor", "st", "matmul", bo.ap[0:64, 0:256],
                          ystg[:, 64 * h:64 * (h + 1)],
                          sb["prepb"][:, 1024 + 256 * c:1024 + 256 * (c + 1)],
                          start=True, stop=True)
                    bs = pf[4]
                    bs.acquire("tensor")
                    for sh in range(2):
                        k.doi("tensor", "st", "matmul", bs.ap[:, 0:64],
                              sb["prepc"][:, 128 * (2 * c + sh):128 * (2 * c + sh) + 128],
                              wb[:, X2 + 64 * sh:X2 + 64 * (sh + 1)],
                              start=(sh == 0), stop=(sh == 1))
                    STall = k.ct["st"]
                    k.wait("vector", "st", STall)
                    Ps = ystg[:, 64 * h:64 * (h + 1)]
                    k.do("vector", "tensor_scalar", out=Ps, in0=Ps,
                         scalar1=sb["rbc"][:, 768 + 4 * h + c:768 + 4 * h + c + 1],
                         scalar2=None, op0=ALU.mult)
                    k.drain("vector")
                    k.do("vector", "tensor_tensor", out=Ps, in0=Ps, in1=bs.ap[:, 0:64],
                         op=ALU.add)
                    svp = k.sig("vector", "sv")
                    bs.release("sv", svp)
                    t2r = t2s[hh]
                    t2r.acquire("vector")
                    k.wait("vector", "dl", RE)
                    k.do("vector", "tensor_tensor", out=t2r.ap, in0=bo.ap[0:64, 0:256],
                         in1=rEa.ap, op=ALU.mult)
                    k.drain("vector")
                    k.do("vector", "tensor_tensor", out=t2r.ap, in0=t2r.ap,
                         in1=bd.ap[0:64, 0:256], op=ALU.add)
                    sve = k.sig("vector", "sv")
                    bd.release("sv", sve)
                    bo.release("sv", sve)
                    rEa.release("sv", sve)
                yr.acquire("gpsimd")
                k.wait("gpsimd", "sv", sve)
                k.dma("gpsimd", "dx", wb[0:64, ys:ys + 256], mam[0:64, 512:768])
                d1 = k.dma("gpsimd", "dx", wb[64:128, ys:ys + 256], mam[0:64, 768:1024])
                t2s[0].release("dx", d1)
                t2s[1].release("dx", d1)
                bz = pf[5]
                bz.acquire("tensor")
                for g4 in range(4):
                    zoff = 4096 + 512 * (g4 % 2) + 1024
                    k.wait("sync", "st", stg4[g4 % 2])
                    for dcl in range(4):
                        dc = 4 * g4 + dcl
                        r, a = dc // 2, dc % 2
                        k.dma("sync", "dw", ystg[:, zoff + 128 * dcl - 1024:zoff + 128 * (dcl + 1) - 1024],
                              AP(wblob.ap().tensor,
                                 r * SHARD_N + IPT + 8512 * 128 * a + 128 * j,
                                 [[8512, 128], [1, 128]]))
                    DWz = k.ct["dw"]
                    k.wait("tensor", "dw", DWz)
                    for dcl in range(4):
                        dc = 4 * g4 + dcl
                        k.doi("tensor", "st", "matmul", bz.ap[:, 0:256],
                              ystg[:, zoff + 128 * dcl - 1024:zoff + 128 * (dcl + 1) - 1024],
                              xnfc(dc, 256 * c, 256), start=(dc == 0), stop=(dc == 15))
                    stg4[g4 % 2] = k.ct["st"]
                STz = k.ct["st"]
                k.wait("scalar", "st", STz)
                k.do("scalar", "activation", out=sb["prepc"][:, 1024:1280],
                     in_=bz.ap[:, 0:256], func=AF.Silu)
                ssz = k.sig("scalar", "ss")
                bz.release("ss", ssz)
                xsl = g[:, 1024 * j + 256 * c:1024 * j + 256 * (c + 1)]
                k.wait("vector", "dx", d1)
                k.do("vector", "tensor_tensor", out=sb["prepc"][:, 1536:1792], in0=xsl,
                     in1=sm[:, 32 + 2 * j:32 + 2 * (j + 1)]
                     .rearrange("p h -> p h ()").broadcast_to([128, 2, 128]),
                     op=ALU.mult)
                k.drain("vector")
                k.do("vector", "tensor_tensor", out=sb["prepc"][:, 1536:1792],
                     in0=sb["prepc"][:, 1536:1792],
                     in1=wb[:, ys:ys + 256], op=ALU.add)
                k.wait("vector", "ss", ssz)
                k.drain("vector")
                k.do("vector", "tensor_tensor", out=xsl, in0=sb["prepc"][:, 1536:1792],
                     in1=sb["prepc"][:, 1024:1280], op=ALU.mult)
                svy = k.sig("vector", "sv")
                yr.release("sv", svy)
                k.wait("scalar", "sv", svy)
                if j > 0:
                    k.wait("scalar", "st")
                k.do("scalar", "activation", out=sb["prepc"][:, 1280:1536], in_=xsl,
                     func=AF.Square)
                ssq = k.sig("scalar", "ss")
                k.wait("tensor", "ss", ssq)
                if j == 0:
                    bq.acquire("tensor")
                k.doi("tensor", "st", "matmul", bq.ap[0:1, 0:256], sb["idn"][:, 128:129],
                      sb["prepc"][:, 1280:1536], start=(j == 0), stop=(j == 31))
            STQ = k.ct["st"]
            k.wait("vector", "st", STQ)
            k.wait("vector", "dg")
            k.do("vector", "tensor_scalar", out=sb["srow"][0:1, 0:256],
                 in0=bq.ap[0:1, 0:256], scalar1=1.0 / D_INNER, scalar2=EPS_LN,
                 op0=ALU.mult, op1=ALU.add)
            k.drain("vector")
            k.do("vector", "reciprocal", out=sb["srow"][0:1, 256:512],
                 in_=sb["srow"][0:1, 0:256])
            svr = k.sig("vector", "sv")
            bq.release("sv", svr)
            k.wait("scalar", "sv", svr)
            k.do("scalar", "activation", out=sb["srow"][0:1, 512:768],
                 in_=sb["srow"][0:1, 256:512], func=AF.Sqrt)
            ssr = k.sig("scalar", "ss")
            k.wait("sync", "ss", ssr)
            k.dma("sync", "dg", dscr_ap(RST_D + 256 * c, [[256, 1], [1, 256]]),
                  sb["srow"][0:1, 512:768])

        k.wait("sync", "dg")
        for tt in range(NT):
            k.dma("sync", "dl", sb["stat"][:, 12 + tt:12 + tt + 1],
                  dscr_ap(RST_D + 128 * tt, [[1, 128], [1, 1]]))
        RSTL = k.wait("vector", "dl")
        wsl4 = [Res(k, sb["wbuf"][:, 512 * i:512 * (i + 1)]) for i in range(4)]
        for dc4 in range(4):
            for tg in range(2):
                banks = [pf[1 + i] for i in range(4)]
                for j in range(32):
                    ws = wsl4[j % 4]
                    ws.acquire("sync")
                    k.dma("sync", "dw", ws.ap,
                          AP(wblob.ap().tensor,
                             (j // 4) * SHARD_N + OPT + 128 * (j % 4) * 2048 + 512 * dc4,
                             [[2048, 128], [1, 512]]))
                    DW = k.ct["dw"]
                    k.wait("tensor", "dw", DW)
                    for tloc in range(4):
                        ti = 4 * tg + tloc
                        if j == 0:
                            banks[tloc].acquire("tensor")
                        k.doi("tensor", "st", "matmul", banks[tloc].ap[:, 0:512],
                              g[:, 1024 * j + 128 * ti:1024 * j + 128 * (ti + 1)],
                              ws.ap, start=(j == 0), stop=(j == 31))
                    ws.release("st", k.ct["st"])
                STm = k.ct["st"]
                for tloc in range(4):
                    ti = 4 * tg + tloc
                    t = sb["prep"][:, 512 * (tloc % 2):512 * (tloc % 2 + 1)]
                    k.wait("vector", "st", STm)
                    k.do("vector", "tensor_scalar", out=t, in0=banks[tloc].ap[:, 0:512],
                         scalar1=sb["stat"][:, 12 + ti:12 + ti + 1], scalar2=None, op0=ALU.mult)
                    k.drain("vector")
                    k.do("vector", "tensor_tensor", out=xt(ti)[:, 512 * dc4:512 * (dc4 + 1)],
                         in0=xt(ti)[:, 512 * dc4:512 * (dc4 + 1)], in1=t, op=ALU.add)
                    svm = k.sig("vector", "sv")
                    banks[tloc].release("sv", svm)
        k.sig("vector", "sv")

    if n_sub >= 1:
        sub_ffn0()
    if n_sub >= 2:
        sub_attn0()
    if n_sub >= 3:
        sub_ffn0()
    if n_sub >= 4:
        sub_ffn1()
    if n_sub >= 5:
        sub_mamba()
    if n_sub >= 6:
        sub_ffn1()

    for semn in ("dl", "dw", "dg", "sv", "ss", "st", "sgp", "scc"):
        k.wait("gpsimd" if dump else "sync", semn)
    if dump:
        k.dma("gpsimd", "dg", out_t[0:128, :], sb["gbuf"][:, 0:2048])
        k.dma("gpsimd", "dg", out_t[128:256, :], sb["gbuf"][:, 16384:18432])
        k.dma("gpsimd", "dg", out_t[256:384, :], sb["prepb"][:, 0:2048])
        k.dma("gpsimd", "dg", out_t[384:512, :], sb["prepc"][:, 0:2048])
        k.dma("gpsimd", "dg", out_t[512:640, :], sb["xb"][:, 0:2048])
        k.dma("gpsimd", "dg", out_t[640:768, 0:1024], sb["mam"][:, 0:1024])
        k.dma("gpsimd", "dg", out_t[768:896, :], sb["ystg"][:, 0:2048])
        k.dma("gpsimd", "dg", out_t[896:1024, :], sb["wbuf"][:, 4096:6144])
        k.wait("gpsimd", "dg")
    else:
        for i in range(NT):
            k.dma("sync", "dg", out_t[128 * i:128 * (i + 1), :], xt(i))
        k.wait("sync", "dg")

    k.finalize(es)
    es.close()
    return nc, taps


def _shard_inputs(inputs, c):
    b, h = c // 2, c % 2
    A = np.ascontiguousarray
    return {
        "x": A(inputs["L0_attn_x"][b, 1024 * h:1024 * (h + 1), :], dtype=np.float32),
        "core_meta": np.array([[h, c] + [0.0] * 14], dtype=np.float32),
        "wfc": A(inputs["L0_ffn_W_fc"][1024 * c:1024 * (c + 1), :], dtype=np.float32),
        "wproj": A(inputs["L0_ffn_W_proj"][:, 1024 * c:1024 * (c + 1)], dtype=np.float32),
        "inproj": A(inputs["L1_attn_in_proj_w"][:, 256 * c:256 * (c + 1)], dtype=np.float32),
        "outproj": A(inputs["L1_attn_out_proj_w"][:, 512 * c:512 * (c + 1)], dtype=np.float32),
        "weight": A(inputs["L1_ffn_weight"][:, 256 * c:256 * (c + 1)], dtype=np.float32),
        "wq": A(inputs["L0_attn_Wq"][:, 256 * c:256 * (c + 1)], dtype=np.float32),
        "wout": A(inputs["L0_attn_Wout"][:, 256 * c:256 * (c + 1)], dtype=np.float32),
        "aw": A(np.asarray(inputs["L1_ffn_A_w"], dtype=np.float32).reshape(128, 2048)),
        "gatew": A(inputs["L1_ffn_gate_w"], dtype=np.float32),
        "wkv": A(inputs["L0_attn_Wkv"], dtype=np.float32),
        "bw": A(np.asarray(inputs["L1_ffn_B_w"], dtype=np.float32).reshape(16384, 16)),
        "gnorm": A(inputs["L0_attn_g_norm"], dtype=np.float32).reshape(1, 2048),
        "gout": A(inputs["L0_attn_g_out"], dtype=np.float32).reshape(1, 2048),
        "normw": A(inputs["L1_attn_norm_w"][512 * c:512 * (c + 1)], dtype=np.float32).reshape(1, 512),
        "convw": A(inputs["L1_attn_conv_w"], dtype=np.float32),
        "dtb": A(inputs["L1_attn_dt_bias"], dtype=np.float32).reshape(64, 1),
        "alog": A(inputs["L1_attn_A_log"], dtype=np.float32).reshape(64, 1),
        "dpar": A(inputs["L1_attn_D_param"], dtype=np.float32).reshape(64, 1),
    }


def kernel(_n_sub=6, _dump=0, **inputs):
    nc, _ = build_core(n_sub=_n_sub, dump=_dump)
    in_maps = [_shard_inputs(inputs, c) for c in range(8)]
    res = run_bass_kernel_spmd(nc, in_maps, core_ids=list(range(8)))
    out = np.zeros((B, S, D), dtype=np.float32)
    for c in range(8):
        out[c // 2, 1024 * (c % 2):1024 * (c % 2 + 1), :] = res.results[c]["out"]
    return out
